# version 01
# baseline (speedup 1.0000x reference)
import numpy as np
import ml_dtypes

import concourse.bass as bass
import concourse.mybir as mybir
from concourse.bass_utils import run_bass_kernel_spmd

F32 = mybir.dt.float32
BF16 = mybir.dt.bfloat16
BF = ml_dtypes.bfloat16
AF = mybir.ActivationFunctionType
ALU = mybir.AluOpType
AX = mybir.AxisListType

NCORE = 8
S = 8192
H = 2048
NH = 16
HD = 128
INTER = 8192
TPC = S // NCORE
HPC = NH // NCORE
NFC = H // 128
NIC = INTER // 128
MASKVAL = -1.0e9
EPS = 1e-5


def _hadamard(n):
    h = np.array([[1.0]], dtype=np.float64)
    while h.shape[0] < n:
        h = np.block([[h, h], [h, -h]])
    return h


def AP2(t, off, dims):
    return bass.AP(t, int(off), [list(d) for d in dims])



def build_l1():
    nc = bass.Bass("TRN2", target_bir_lowering=False)
    x_in = nc.dram_tensor("x", [TPC, H], F32, kind="ExternalInput")
    xn_out = nc.dram_tensor("xn_fm", [H, TPC], BF16, kind="ExternalOutput")
    ident = nc.inline_tensor(np.eye(128, dtype=BF), name="ident")

    ntc = TPC // 128
    with (
        nc.sbuf_tensor("xbuf", [128, 2 * H], F32) as xbuf,
        nc.sbuf_tensor("sq", [128, H], F32) as sq,
        nc.sbuf_tensor("stats", [128, 8], F32) as stats,
        nc.sbuf_tensor("xn_tm", [128, 2 * H], BF16) as xn_tm,
        nc.sbuf_tensor("ident_sb", [128, 128], BF16) as ident_sb,
        nc.sbuf_tensor("xnf_sb", [128, NFC * TPC], BF16) as xn_fm,
        nc.psum_tensor("tpA", [128, 1024], BF16) as tpA,
        nc.psum_tensor("tpB", [128, 1024], BF16) as tpB,
        nc.semaphore("ld0") as ld0,
        nc.semaphore("ld1") as ld1,
        nc.semaphore("idl") as idl,
        nc.semaphore("vst") as vst,
        nc.semaphore("sst") as sst,
        nc.semaphore("tmm") as tmm,
        nc.semaphore("tev") as tev,
        nc.semaphore("vsy") as vsy,
        nc.semaphore("st") as st,
        nc.Block() as block,
    ):
        lds = [ld0, ld1]
        tp = [tpA, tpB]

        @block.sync
        def _(s):
            s.dma_start(out=ident_sb[:, :], in_=ident.ap()).then_inc(idl, 16)
            for tc in range(ntc):
                if tc >= 2:
                    s.wait_ge(tmm, (tc - 1) * 16)
                s.dma_start(out=xbuf[:, (tc % 2) * H:(tc % 2) * H + H],
                            in_=x_in[tc * 128:(tc + 1) * 128, :]
                            ).then_inc(lds[tc % 2], 16)

        @block.scalar
        def _(sc):
            for tc in range(ntc):
                sl = (tc % 2) * H
                sc.wait_ge(lds[tc % 2], (tc // 2 + 1) * 16)
                sc.activation(sq[:, :], xbuf[:, sl:sl + H], AF.Square,
                              accum_out=stats[:, 2:3]).then_inc(sst, 1)
                sc.wait_ge(vst, 5 * tc + 3)
                sc.activation(stats[:, 4:5], stats[:, 3:4], AF.Sqrt).then_inc(sst, 1)

        @block.vector
        def _(v):
            vsn = [0]

            def vsy_inc(inst):
                vsn[0] += 1
                inst.then_inc(vsy, 1)

            def vsy_wait():
                v.wait_ge(vsy, vsn[0])

            for tc in range(ntc):
                sl = (tc % 2) * H
                v.wait_ge(lds[tc % 2], (tc // 2 + 1) * 16)
                vsy_inc(v.reduce_sum(stats[:, 0:1], xbuf[:, sl:sl + H], axis=AX.X))
                v.wait_ge(sst, 2 * tc + 1)
                vsy_wait()
                vsy_inc(v.tensor_scalar_mul(stats[:, 1:2], stats[:, 0:1], 1.0 / H))
                vsy_wait()
                v.tensor_mul(stats[:, 5:6], stats[:, 1:2], stats[:, 1:2])
                vsy_inc(v.tensor_scalar(stats[:, 3:4], stats[:, 2:3], 1.0 / H, EPS,
                                        ALU.mult, ALU.add))
                vsy_wait()
                v.tensor_sub(stats[:, 3:4], stats[:, 3:4], stats[:, 5:6]
                             ).then_inc(vst, 3)
                v.wait_ge(sst, 2 * tc + 2)
                vsy_inc(v.reciprocal(stats[:, 6:7], stats[:, 4:5]))
                vsy_wait()
                v.tensor_scalar(xn_tm[:, sl:sl + H], xbuf[:, sl:sl + H],
                                stats[:, 1:2], stats[:, 6:7],
                                ALU.subtract, ALU.mult).then_inc(vst, 2)
                if tc >= 1:
                    for half in range(2):
                        v.wait_ge(tmm, (tc - 1) * 16 + (half + 1) * 8)
                        dst = AP2(xn_fm, (half * 8) * TPC + (tc - 1) * 128,
                                  [[NFC * TPC, 128], [TPC, 8], [1, 128]])
                        v.tensor_copy(dst, tp[half][:, 0:1024]).then_inc(tev, 1)
            for half in range(2):
                v.wait_ge(tmm, (ntc - 1) * 16 + (half + 1) * 8)
                dst = AP2(xn_fm, (half * 8) * TPC + (ntc - 1) * 128,
                          [[NFC * TPC, 128], [TPC, 8], [1, 128]])
                v.tensor_copy(dst, tp[half][:, 0:1024]).then_inc(tev, 1)

        @block.tensor
        def _(t):
            t.wait_ge(idl, 16)
            for tc in range(ntc):
                sl = (tc % 2) * H
                t.wait_ge(vst, 5 * tc + 5)
                for half in range(2):
                    if tc >= 1:
                        t.wait_ge(tev, (tc - 1) * 2 + half + 1)
                    for j in range(8):
                        fc = half * 8 + j
                        inst = t.transpose(tp[half][:, j * 128:(j + 1) * 128],
                                           xn_tm[:, sl + fc * 128:sl + (fc + 1) * 128],
                                           ident_sb[:, :])
                    inst.then_inc(tmm, 8)

        @block.sync
        def _(s):
            s.wait_ge(tev, ntc * 2)
            s.dma_start(out=AP2(xn_out, 0, [[TPC, 128], [128 * TPC, NFC], [1, TPC]]),
                        in_=AP2(xn_fm, 0, [[NFC * TPC, 128], [TPC, NFC], [1, TPC]])
                        ).then_inc(st, 16)
            s.wait_ge(st, 16)
    return nc





def _ffn_persist_alloc(nc, es):
    d = {}
    d["tbl_ug"] = es.enter_context(nc.sbuf_tensor("tbl_ug", [128, NFC * 128], BF16))
    d["tbl_dp"] = es.enter_context(nc.sbuf_tensor("tbl_dp", [128, NIC * 128], BF16))
    d["wsg"] = es.enter_context(nc.sbuf_tensor("wsg", [128, NIC], F32))
    d["wsdp_bc"] = es.enter_context(nc.sbuf_tensor("wsdp_bc", [128, H], F32))
    d["ones_k1"] = es.enter_context(nc.sbuf_tensor("ones_k1", [1, 128], F32))
    d["y_had"] = es.enter_context(nc.sbuf_tensor("y_had", [128, NFC * TPC], BF16))
    return d


def emit_ffn_tables(nc, d, su_ug2, su_dp2, wsu2, wsg2, wscale_dp, h128f):
    with (
        nc.sbuf_tensor("t_suug", [128, NFC], F32) as suug,
        nc.sbuf_tensor("t_sudp", [128, NIC], F32) as sudp,
        nc.sbuf_tensor("t_wsu", [128, NIC], F32) as wsu,
        nc.sbuf_tensor("t_wsdprow", [1, H], F32) as wsdprow,
        nc.sbuf_tensor("t_h128", [128, 128], F32) as h128,
        nc.sbuf_tensor("t_scomb", [128, NIC], F32) as scomb,
        nc.psum_tensor("t_bc", [128, H], F32) as bcp,
        nc.semaphore("tb_ld") as ld,
        nc.semaphore("tb_v") as vs,
        nc.semaphore("tb_m") as mm,
    ):
        with nc.Block() as block:
            @block.sync
            def _(s):
                s.dma_start(out=suug[:, :], in_=su_ug2[:, :]).then_inc(ld, 16)
                s.dma_start(out=sudp[:, :], in_=su_dp2[:, :]).then_inc(ld, 16)
                s.dma_start(out=wsu[:, :], in_=wsu2[:, :]).then_inc(ld, 16)
                s.dma_start(out=d["wsg"][:, :], in_=wsg2[:, :]).then_inc(ld, 16)
                s.dma_start(out=wsdprow[:, :], in_=wscale_dp[:]).then_inc(ld, 16)
                s.dma_start(out=h128[:, :], in_=h128f.ap()).then_inc(ld, 16)

            @block.vector
            def _(v):
                v.memset(d["ones_k1"][:, :], 1.0)
                v.wait_ge(ld, 96)
                vseq = [0]

                def stamp(inst):
                    vseq[0] += 1
                    inst.then_inc(vs, 1)

                stamp(v.tensor_scalar_mul(suug[:, :], suug[:, :], 1.0 / np.sqrt(2048.0)))
                v.wait_ge(vs, vseq[0])
                for c in range(NFC):
                    v.tensor_scalar(d["tbl_ug"][:, c * 128:(c + 1) * 128], h128[:, :],
                                    suug[:, c:c + 1], None, ALU.mult)
                stamp(v.tensor_mul(scomb[:, :], sudp[:, :], wsu[:, :]))
                v.wait_ge(vs, vseq[0])
                stamp(v.tensor_scalar_mul(scomb[:, :], scomb[:, :], 1.0 / np.sqrt(8192.0)))
                v.wait_ge(vs, vseq[0])
                for c in range(NIC):
                    v.tensor_scalar(d["tbl_dp"][:, c * 128:(c + 1) * 128], h128[:, :],
                                    scomb[:, c:c + 1], None, ALU.mult)

        with nc.Block() as block:
            @block.tensor
            def _(t):
                for j in range(4):
                    t.matmul(bcp[:, j * 512:(j + 1) * 512], d["ones_k1"][:, :],
                             wsdprow[:, j * 512:(j + 1) * 512],
                             start=True, stop=True).then_inc(mm, 1)

            @block.vector
            def _(v):
                v.wait_ge(mm, 4)
                v.tensor_copy(d["wsdp_bc"][:, :], bcp[:, :])


def emit_ffn_yhad(nc, d, xn_own_sb):
    with (
        nc.sbuf_tensor("yz_a", [128, NFC * 512], F32) as za,
        nc.sbuf_tensor("yz_b", [128, NFC * 512], F32) as zb,
        nc.psum_tensor("y_ps0", [128, 512], F32) as yp0,
        nc.psum_tensor("y_ps1", [128, 512], F32) as yp1,
        nc.semaphore("yh_m") as mm,
        nc.semaphore("yh_e") as ev,
        nc.semaphore("yh_v") as vs,
        nc.Block() as block,
    ):
        yp = [yp0, yp1]

        @block.tensor
        def _(t):
            for th in range(2):
                for c in range(NFC):
                    i = th * NFC + c
                    if i >= 2:
                        t.wait_ge(ev, i - 1)
                    t.matmul(yp[i % 2][:, :], d["tbl_ug"][:, c * 128:(c + 1) * 128],
                             xn_own_sb[:, c * TPC + th * 512: c * TPC + th * 512 + 512],
                             start=True, stop=True).then_inc(mm, 1)

        @block.vector
        def _(v):
            vseq = [0]

            def stamp(inst):
                vseq[0] += 1
                inst.then_inc(vs, 1)

            for th in range(2):
                for c in range(NFC):
                    i = th * NFC + c
                    v.wait_ge(mm, i + 1)
                    v.tensor_copy(za[:, c * 512:(c + 1) * 512], yp[i % 2][:, :]
                                  ).then_inc(ev, 1)
                v.wait_ge(ev, (th + 1) * NFC)
                src, dst = za, zb
                for si, dd in enumerate([1, 2, 4, 8]):
                    last = si == 3
                    for hi in range(2):
                        s_off = (dd if hi else 0) * 512
                        d_off = (dd if hi else 0) * 512
                        sA = AP2(src, s_off, [[NFC * 512, 128], [2 * dd * 512, NFC // (2 * dd)], [512, dd], [1, 512]])
                        sB = AP2(src, s_off + (512 * dd if not hi else -512 * dd),
                                 [[NFC * 512, 128], [2 * dd * 512, NFC // (2 * dd)], [512, dd], [1, 512]])
                        if last:
                            dst_ap = AP2(d["y_had"], (dd if hi else 0) * TPC + th * 512,
                                         [[NFC * TPC, 128], [2 * dd * TPC, NFC // (2 * dd)], [TPC, dd], [1, 512]])
                        else:
                            dst_ap = AP2(dst, d_off, [[NFC * 512, 128], [2 * dd * 512, NFC // (2 * dd)], [512, dd], [1, 512]])
                        op = ALU.add if not hi else ALU.subtract
                        if hi:
                            stamp(v.tensor_tensor(dst_ap, sB, sA, ALU.subtract))
                        else:
                            stamp(v.tensor_tensor(dst_ap, sA, sB, ALU.add))
                    v.wait_ge(vs, vseq[0])
                    src, dst = dst, src


def emit_ffn_main(nc, d, w_upT, w_gateT, w_downT, ffn_out):
    WB = 256
    NB = INTER // WB
    HB = 256
    NHB = H // HB
    for th in range(2):
        with nc.sbuf_tensor(f"f_h{th}", [128, NIC * 512], BF16) as hbuf:
            with (
                nc.sbuf_tensor(f"f_wu{th}", [128, 2 * NFC * WB], BF16) as wub,
                nc.sbuf_tensor(f"f_wg{th}", [128, 2 * NFC * WB], BF16) as wgb,
                nc.sbuf_tensor(f"f_g{th}", [128, 2 * 512], F32) as gbuf,
                nc.psum_tensor(f"f_pu0{th}", [128, 512], F32) as pu0,
                nc.psum_tensor(f"f_pu1{th}", [128, 512], F32) as pu1,
                nc.psum_tensor(f"f_pg0{th}", [128, 512], F32) as pg0,
                nc.psum_tensor(f"f_pg1{th}", [128, 512], F32) as pg1,
                nc.semaphore(f"f_wlu0{th}") as wlu0,
                nc.semaphore(f"f_wlu1{th}") as wlu1,
                nc.semaphore(f"f_wlg0{th}") as wlg0,
                nc.semaphore(f"f_wlg1{th}") as wlg1,
                nc.semaphore(f"f_mmu{th}") as mmu,
                nc.semaphore(f"f_mmg{th}") as mmg,
                nc.semaphore(f"f_sil{th}") as sil,
                nc.semaphore(f"f_hdn{th}") as hdn,
                nc.Block() as block,
            ):
                pu, pg = [pu0, pu1], [pg0, pg1]
                wlu, wlg = [wlu0, wlu1], [wlg0, wlg1]

                @block.gpsimd
                def _(g):
                    for ib in range(NB):
                        slot = ib % 2
                        if ib >= 2:
                            g.wait_ge(mmg, 2 * (ib - 1))
                        g.dma_start(
                            out=AP2(wub, slot * NFC * WB,
                                    [[2 * NFC * WB, 128], [WB, NFC], [1, WB]]),
                            in_=AP2(w_upT, ib * WB,
                                    [[INTER, 128], [128 * INTER, NFC], [1, WB]])
                        ).then_inc(wlu[slot], 16)
                        g.dma_start(
                            out=AP2(wgb, slot * NFC * WB,
                                    [[2 * NFC * WB, 128], [WB, NFC], [1, WB]]),
                            in_=AP2(w_gateT, ib * WB,
                                    [[INTER, 128], [128 * INTER, NFC], [1, WB]])
                        ).then_inc(wlg[slot], 16)

                @block.tensor
                def _(t):
                    for c in range(NIC):
                        ib, ps, slot = c // 2, c % 2, (c // 2) % 2
                        t.wait_ge(wlu[slot], 16 * (ib // 2 + 1))
                        t.wait_ge(wlg[slot], 16 * (ib // 2 + 1))
                        if c >= 2:
                            t.wait_ge(hdn, c - 1)
                            t.wait_ge(sil, c - 1)
                        base = slot * NFC * WB + (c % 2) * 128
                        for hc in range(NFC):
                            inst = t.matmul(pu[ps][:, :],
                                            wub[:, base + hc * WB: base + hc * WB + 128],
                                            d["y_had"][:, hc * TPC + th * 512: hc * TPC + th * 512 + 512],
                                            start=(hc == 0), stop=(hc == NFC - 1))
                        inst.then_inc(mmu, 1)
                        for hc in range(NFC):
                            inst = t.matmul(pg[ps][:, :],
                                            wgb[:, base + hc * WB: base + hc * WB + 128],
                                            d["y_had"][:, hc * TPC + th * 512: hc * TPC + th * 512 + 512],
                                            start=(hc == 0), stop=(hc == NFC - 1))
                        inst.then_inc(mmg, 1)

                @block.scalar
                def _(sc):
                    for c in range(NIC):
                        sc.wait_ge(mmg, c + 1)
                        if c >= 2:
                            sc.wait_ge(hdn, c - 1)
                        sc.activation(gbuf[:, (c % 2) * 512:(c % 2) * 512 + 512],
                                      pg[c % 2][:, :], AF.Silu,
                                      scale=d["wsg"][:, c:c + 1]).then_inc(sil, 1)

                @block.vector
                def _(v):
                    for c in range(NIC):
                        v.wait_ge(mmu, c + 1)
                        v.wait_ge(sil, c + 1)
                        v.tensor_mul(hbuf[:, c * 512:(c + 1) * 512],
                                     gbuf[:, (c % 2) * 512:(c % 2) * 512 + 512],
                                     pu[c % 2][:, :]).then_inc(hdn, 1)

            with (
                nc.sbuf_tensor(f"f_z1a{th}", [128, NIC * 128], F32) as z1a,
                nc.sbuf_tensor(f"f_z1b{th}", [128, NIC * 128], F32) as z1b,
                nc.psum_tensor(f"f_s1p0{th}", [128, 512], F32) as s1p0,
                nc.psum_tensor(f"f_s1p1{th}", [128, 512], F32) as s1p1,
                nc.semaphore(f"f_s1m{th}") as s1m,
                nc.semaphore(f"f_s1e{th}") as s1e,
                nc.semaphore(f"f_bfv{th}") as bfv,
                nc.Block() as block,
            ):
                s1p = [s1p0, s1p1]

                @block.tensor
                def _(t):
                    for e in range(4):
                        for grp in range(16):
                            gg = e * 16 + grp
                            if gg >= 2:
                                t.wait_ge(s1e, gg - 1)
                            for k in range(4):
                                c = grp * 4 + k
                                inst = t.matmul(
                                    s1p[gg % 2][:, k * 128:(k + 1) * 128],
                                    d["tbl_dp"][:, c * 128:(c + 1) * 128],
                                    hbuf[:, c * 512 + e * 128: c * 512 + e * 128 + 128],
                                    start=True, stop=True)
                            inst.then_inc(s1m, 1)

                @block.scalar
                def _(sc):
                    for e in range(4):
                        for grp in range(16):
                            gg = e * 16 + grp
                            sc.wait_ge(s1m, gg + 1)
                            if e > 0 and grp == 0:
                                sc.wait_ge(bfv, 6 * e)
                            sc.activation(z1a[:, grp * 512:(grp + 1) * 512],
                                          s1p[gg % 2][:, :], AF.Copy).then_inc(s1e, 1)

                @block.vector
                def _(v):
                    for e in range(4):
                        v.wait_ge(s1e, (e + 1) * 16)
                        src, dst = z1a, z1b
                        for si, dd in enumerate([1, 2, 4, 8, 16, 32]):
                            last = si == 5
                            nq = NIC // (2 * dd)
                            loA = AP2(src, 0, [[NIC * 128, 128], [2 * dd * 128, nq], [128, dd], [1, 128]])
                            hiA = AP2(src, dd * 128, [[NIC * 128, 128], [2 * dd * 128, nq], [128, dd], [1, 128]])
                            if last:
                                dlo = AP2(hbuf, e * 128, [[NIC * 512, 128], [2 * dd * 512, nq], [512, dd], [1, 128]])
                                dhi = AP2(hbuf, dd * 512 + e * 128, [[NIC * 512, 128], [2 * dd * 512, nq], [512, dd], [1, 128]])
                            else:
                                dlo = AP2(dst, 0, [[NIC * 128, 128], [2 * dd * 128, nq], [128, dd], [1, 128]])
                                dhi = AP2(dst, dd * 128, [[NIC * 128, 128], [2 * dd * 128, nq], [128, dd], [1, 128]])
                            v.tensor_tensor(dlo, loA, hiA, ALU.add)
                            v.tensor_tensor(dhi, loA, hiA, ALU.subtract).then_inc(bfv, 1)
                            v.wait_ge(bfv, e * 6 + si + 1)
                            src, dst = dst, src

            with (
                nc.sbuf_tensor(f"f_wd{th}", [128, 2 * NIC * HB], BF16) as wdn,
                nc.sbuf_tensor(f"f_slab{th}", [128, 2 * HB], F32) as slab,
                nc.psum_tensor(f"f_dp0{th}", [128, 512], F32) as dp0,
                nc.psum_tensor(f"f_dp1{th}", [128, 512], F32) as dp1,
                nc.semaphore(f"f_wdl0{th}") as wdl0,
                nc.semaphore(f"f_wdl1{th}") as wdl1,
                nc.semaphore(f"f_dmm{th}") as dmm,
                nc.semaphore(f"f_dev{th}") as dev,
                nc.semaphore(f"f_fst0{th}") as fst0,
                nc.semaphore(f"f_fst1{th}") as fst1,
                nc.Block() as block,
            ):
                dp = [dp0, dp1]
                wdl = [wdl0, wdl1]
                fst = [fst0, fst1]

                @block.gpsimd
                def _(g):
                    for hb in range(NHB):
                        slot = hb % 2
                        if hb >= 2:
                            g.wait_ge(dmm, 4 * (hb - 1))
                        g.dma_start(
                            out=AP2(wdn, slot * NIC * HB,
                                    [[2 * NIC * HB, 128], [HB, NIC], [1, HB]]),
                            in_=AP2(w_downT, hb * HB,
                                    [[H, 128], [128 * H, NIC], [1, HB]])
                        ).then_inc(wdl[slot], 16)

                @block.tensor
                def _(t):
                    for hb in range(NHB):
                        slot = hb % 2
                        t.wait_ge(wdl[slot], 16 * (hb // 2 + 1))
                        for tc in range(4):
                            i = hb * 4 + tc
                            if i >= 2:
                                t.wait_ge(dev, i - 1)
                            for c in range(NIC):
                                inst = t.matmul(
                                    dp[i % 2][:, 0:HB],
                                    hbuf[:, c * 512 + tc * 128: c * 512 + tc * 128 + 128],
                                    wdn[:, slot * NIC * HB + c * HB: slot * NIC * HB + (c + 1) * HB],
                                    start=(c == 0), stop=(c == NIC - 1))
                            inst.then_inc(dmm, 1)

                @block.vector
                def _(v):
                    for i in range(4 * NHB):
                        hb = i // 4
                        v.wait_ge(dmm, i + 1)
                        if i >= 2:
                            v.wait_ge(fst[i % 2], 16 * (i // 2))
                        v.tensor_mul(slab[:, (i % 2) * HB:(i % 2) * HB + HB],
                                     dp[i % 2][:, 0:HB],
                                     d["wsdp_bc"][:, hb * HB:(hb + 1) * HB]).then_inc(dev, 1)

                @block.sync
                def _(s):
                    for i in range(4 * NHB):
                        hb, tc = i // 4, i % 4
                        s.wait_ge(dev, i + 1)
                        s.dma_start(
                            out=ffn_out[th * 512 + tc * 128: th * 512 + (tc + 1) * 128,
                                        hb * HB:(hb + 1) * HB],
                            in_=slab[:, (i % 2) * HB:(i % 2) * HB + HB]
                        ).then_inc(fst[i % 2], 16)
                    s.wait_ge(fst[0], 16 * 2 * NHB)
                    s.wait_ge(fst[1], 16 * 2 * NHB)


def build_ffn():
    nc = bass.Bass("TRN2", target_bir_lowering=False)
    xn_in = nc.dram_tensor("xn_own", [H, TPC], BF16, kind="ExternalInput")
    su_ug2 = nc.dram_tensor("su_ug2", [128, NFC], F32, kind="ExternalInput")
    su_dp2 = nc.dram_tensor("su_dp2", [128, NIC], F32, kind="ExternalInput")
    wsu2 = nc.dram_tensor("wsu2", [128, NIC], F32, kind="ExternalInput")
    wsg2 = nc.dram_tensor("wsg2", [128, NIC], F32, kind="ExternalInput")
    ws_dp = nc.dram_tensor("ws_dp", [H], F32, kind="ExternalInput")
    w_upT = nc.dram_tensor("w_upT", [H, INTER], F32, kind="ExternalInput")
    w_gateT = nc.dram_tensor("w_gateT", [H, INTER], F32, kind="ExternalInput")
    w_downT = nc.dram_tensor("w_downT", [INTER, H], F32, kind="ExternalInput")
    ffn_out = nc.dram_tensor("ffn_out", [TPC, H], F32, kind="ExternalOutput")
    h128f = nc.inline_tensor(_hadamard(128).astype(np.float32), name="h128f")

    import contextlib
    with contextlib.ExitStack() as es:
        d = _ffn_persist_alloc(nc, es)
        with nc.sbuf_tensor("xn_sb", [128, NFC * TPC], BF16) as xn_sb:
            with (
                nc.semaphore("xl") as xl,
                nc.Block() as block,
            ):
                @block.sync
                def _(s):
                    s.dma_start(out=AP2(xn_sb, 0, [[NFC * TPC, 128], [TPC, NFC], [1, TPC]]),
                                in_=AP2(xn_in, 0, [[TPC, 128], [128 * TPC, NFC], [1, TPC]])
                                ).then_inc(xl, 16)
                    s.wait_ge(xl, 16)
            emit_ffn_tables(nc, d, su_ug2, su_dp2, wsu2, wsg2, ws_dp, h128f)
            emit_ffn_yhad(nc, d, xn_sb)
        emit_ffn_main(nc, d, w_upT, w_gateT, w_downT, ffn_out)
    return nc



def emit_attention(nc, xn_full, wqkv, bqkv2, ctx_out, mask_inl, q_sb, k_sb, v_tm, ctx_sb,
                   debug_qkv=None):
    NTB = S // 512
    NOC = HPC * 3
    import contextlib
    with contextlib.ExitStack() as qes:
        E = qes.enter_context
        wq_sb = E(nc.sbuf_tensor("a_wqkv", [128, NFC * 768], BF16))
        bq_sb = E(nc.sbuf_tensor("a_bqkv", [128, 6], F32))
        xnb = E(nc.sbuf_tensor("a_xnb", [128, 2 * NFC * 512], BF16))
        vstage = E(nc.sbuf_tensor("a_vst", [128, 2 * 512], BF16))
        wl = E(nc.semaphore("a_wl"))
        wl2 = E(nc.semaphore("a_wl2"))
        xl0 = E(nc.semaphore("a_xl0"))
        xl1 = E(nc.semaphore("a_xl1"))
        qmm = E(nc.semaphore("a_qmm"))
        qev = E(nc.semaphore("a_qev"))
        vtm = E(nc.semaphore("a_vtm"))
        vte = E(nc.semaphore("a_vte"))
        qp0 = E(nc.psum_tensor("a_qp0", [128, 512], F32))
        qp1 = E(nc.psum_tensor("a_qp1", [128, 512], F32))
        qp2 = E(nc.psum_tensor("a_qp2", [128, 512], F32))
        tp0 = E(nc.psum_tensor("a_tp0", [128, 1024], BF16))
        tp1 = E(nc.psum_tensor("a_tp1", [128, 1024], BF16))
        id_sb = E(nc.sbuf_tensor("a_id", [128, 128], BF16))
        block = E(nc.Block())
        qp = [qp0, qp1, qp2]
        tpp = [tp0, tp1]
        xl = [xl0, xl1]

        @block.gpsimd
        def _(g):
            g.dma_start(out=AP2(wq_sb, 0, [[NFC * 768, 128], [768, NFC], [1, 768]]),
                        in_=AP2(wqkv, 0, [[768, 128], [128 * 768, NFC], [1, 768]])
                        ).then_inc(wl2, 16)

        @block.sync
        def _(s):
            s.dma_start(out=bq_sb[:, :], in_=bqkv2[:, :]).then_inc(wl, 16)
            s.dma_start(out=id_sb[:, :],
                        in_=nc_ident_ap(nc)).then_inc(wl, 16)
            for tb in range(NTB):
                if tb >= 2:
                    s.wait_ge(qmm, (tb - 1) * NOC)
                r, half = tb // 2, tb % 2
                s.dma_start(
                    out=AP2(xnb, (tb % 2) * NFC * 512,
                            [[2 * NFC * 512, 128], [512, NFC], [1, 512]]),
                    in_=AP2(xn_full, r * H * TPC + half * 512,
                            [[TPC, 128], [128 * TPC, NFC], [1, 512]])
                ).then_inc(xl[tb % 2], 16)

        @block.tensor
        def _(t):
            t.wait_ge(wl, 32)
            t.wait_ge(wl2, 16)
            for tb in range(NTB):
                slot = tb % 2
                t.wait_ge(xl[slot], 16 * (tb // 2 + 1))
                for oc in range(NOC):
                    i = tb * NOC + oc
                    if i >= 3:
                        t.wait_ge(qev, i - 2)
                    for fc in range(NFC):
                        inst = t.matmul(
                            qp[i % 3][:, :],
                            wq_sb[:, fc * 768 + oc * 128: fc * 768 + (oc + 1) * 128],
                            xnb[:, slot * NFC * 512 + fc * 512: slot * NFC * 512 + (fc + 1) * 512],
                            start=(fc == 0), stop=(fc == NFC - 1))
                    inst.then_inc(qmm, 1)
                for h in range(HPC):
                    t.wait_ge(qev, tb * NOC + 3 * h + 3)
                    if tb >= 1:
                        t.wait_ge(vte, (tb - 1) * 2 + h + 1)
                    for j in range(4):
                        inst = t.transpose(tpp[h][:, j * 128:(j + 1) * 128],
                                           vstage[:, h * 512 + j * 128: h * 512 + (j + 1) * 128],
                                           id_sb[:, :])
                    inst.then_inc(vtm, 1)

        @block.scalar
        def _(sc):
            for tb in range(NTB):
                for oc in range(NOC):
                    i = tb * NOC + oc
                    h, kind = oc // 3, oc % 3
                    sc.wait_ge(qmm, i + 1)
                    if kind == 0:
                        dst = q_sb[:, h * S + tb * 512: h * S + (tb + 1) * 512]
                    elif kind == 1:
                        dst = k_sb[:, h * S + tb * 512: h * S + (tb + 1) * 512]
                    else:
                        if tb >= 1:
                            sc.wait_ge(vtm, (tb - 1) * 2 + h + 1)
                        dst = vstage[:, h * 512:(h + 1) * 512]
                    sc.activation(dst, qp[i % 3][:, :], AF.Identity,
                                  bias=bq_sb[:, oc:oc + 1]).then_inc(qev, 1)

        @block.vector
        def _(v):
            for tb in range(NTB):
                for h in range(HPC):
                    v.wait_ge(vtm, tb * 2 + h + 1)
                    v.tensor_copy(v_tm[:, h * S + tb * 512: h * S + (tb + 1) * 512],
                                  tpp[h][:, 0:512]).then_inc(vte, 1)

    if debug_qkv is not None:
        dq, dk, dv = debug_qkv
        with nc.semaphore("dbg_st") as dst_sem, nc.Block() as block:
            @block.sync
            def _(s):
                s.dma_start(out=dq[:, :], in_=q_sb[:, :]).then_inc(dst_sem, 16)
                s.dma_start(out=dk[:, :], in_=k_sb[:, :]).then_inc(dst_sem, 16)
                s.dma_start(out=dv[:, :], in_=v_tm[:, :]).then_inc(dst_sem, 16)
                s.wait_ge(dst_sem, 48)
        return

    import contextlib
    with contextlib.ExitStack() as aes:
        E = aes.enter_context
        mask_sb = E(nc.sbuf_tensor("a_mask", [128, 1024], F32))
        pt_sb = E(nc.sbuf_tensor("a_pt", [128, 4 * 512], BF16))
        ones_col = E(nc.sbuf_tensor("a_ones", [128, 1], BF16))
        ones_k1 = E(nc.sbuf_tensor("a_onesk1", [1, 128], F32))
        l_sb = E(nc.sbuf_tensor("a_l", [1, 512], F32))
        l_inv = E(nc.sbuf_tensor("a_linv", [1, 512], F32))
        lbc_sb = E(nc.sbuf_tensor("a_lbc", [128, 512], F32))
        sp0 = E(nc.psum_tensor("a_sp0", [128, 512], F32))
        sp1 = E(nc.psum_tensor("a_sp1", [128, 512], F32))
        sp2 = E(nc.psum_tensor("a_sp2", [128, 512], F32))
        ctx_ps = E(nc.psum_tensor("a_ctxp", [128, 512], F32))
        l_ps = E(nc.psum_tensor("a_lp", [128, 512], F32))
        lbc_ps = E(nc.psum_tensor("a_lbcp", [128, 512], F32))
        ml = E(nc.semaphore("a_ml"))
        amm = E(nc.semaphore("a_amm"))
        aex = E(nc.semaphore("a_aex"))
        amk = E(nc.semaphore("a_amk"))
        apv = E(nc.semaphore("a_apv"))
        alv = E(nc.semaphore("a_alv"))
        albc = E(nc.semaphore("a_albc"))
        alev = E(nc.semaphore("a_alev"))
        acev = E(nc.semaphore("a_acev"))
        cst = E(nc.semaphore("a_cst"))
        block = E(nc.Block())
        sp = [sp0, sp1, sp2]
        groups = [(h, g) for h in range(HPC) for g in range(16)]

        gj_of = {}
        mask_count_before = {}
        gj = 0
        nmask = 0
        for (h, g) in groups:
            kmax = 4 * (g + 1)
            ids = []
            for j in range(kmax):
                diag = j >= kmax - 4
                ids.append((gj, j, diag, nmask))
                if diag:
                    nmask += 1
                gj += 1
            gj_of[(h, g)] = ids
        total_pv = gj

        @block.sync
        def _(s):
            s.dma_start(out=mask_sb[:, :], in_=mask_inl.ap()).then_inc(ml, 16)

        @block.vector
        def _(v):
            v.memset(ones_col[:, :], 1.0)
            v.memset(ones_k1[:, :], 1.0).then_inc(ml, 1)

        def _emit_pv(t, gi, rec, h, kmax):
            (cj, j, diag, _mc) = rec
            t.wait_ge(aex, cj + 1)
            if j == 0 and gi >= 1:
                t.wait_ge(acev, gi)
            t.matmul(ctx_ps[:, :],
                     v_tm[:, h * S + j * 128: h * S + (j + 1) * 128],
                     pt_sb[:, (cj % 4) * 512:(cj % 4 + 1) * 512],
                     start=(j == 0), stop=(j == kmax - 1))
            t.matmul(l_ps[0:1, :], ones_col[:, :],
                     pt_sb[:, (cj % 4) * 512:(cj % 4 + 1) * 512],
                     start=(j == 0), stop=(j == kmax - 1)).then_inc(apv, 1)

        @block.tensor
        def _(t):
            t.wait_ge(ml, 17)
            for gi, (h, g) in enumerate(groups):
                ids = gj_of[(h, g)]
                kmax = len(ids)
                for (cj, j, diag, _mc) in ids:
                    if cj >= 3:
                        t.wait_ge(aex, cj - 2)
                    t.matmul(sp[cj % 3][:, :],
                             k_sb[:, h * S + j * 128: h * S + (j + 1) * 128],
                             q_sb[:, h * S + g * 512: h * S + (g + 1) * 512],
                             start=True, stop=True).then_inc(amm, 1)
                    if j >= 1:
                        _emit_pv(t, gi, ids[j - 1], h, kmax)
                _emit_pv(t, gi, ids[kmax - 1], h, kmax)
                t.wait_ge(alv, 2 * gi + 2)
                t.wait_ge(alev, gi)
                t.matmul(lbc_ps[:, :], ones_k1[:, :], l_inv[:, :],
                         start=True, stop=True).then_inc(albc, 1)

        @block.vector
        def _(v):
            for gi, (h, g) in enumerate(groups):
                ids = gj_of[(h, g)]
                for (cj, j, diag, mc) in ids:
                    if diag:
                        jloc = j - (len(ids) - 4)
                        u0 = 512 - 128 * jloc
                        v.wait_ge(amm, cj + 1)
                        v.tensor_add(sp[cj % 3][:, :], sp[cj % 3][:, :],
                                     mask_sb[:, u0:u0 + 512]).then_inc(amk, 1)
                v.wait_ge(apv, ids[-1][0] + 1)
                v.tensor_copy(l_sb[:, :], l_ps[0:1, :]).then_inc(alv, 1)
                v.wait_ge(alv, 2 * gi + 1)
                v.reciprocal(l_inv[:, :], l_sb[:, :]).then_inc(alv, 1)
                v.wait_ge(albc, gi + 1)
                v.tensor_copy(lbc_sb[:, :], lbc_ps[:, :]).then_inc(alev, 1)
                v.wait_ge(alev, gi + 1)
                v.tensor_mul(ctx_sb[:, h * S + g * 512: h * S + (g + 1) * 512],
                             ctx_ps[:, :], lbc_sb[:, :]).then_inc(acev, 1)

        @block.scalar
        def _(sc):
            for gi, (h, g) in enumerate(groups):
                for (cj, j, diag, mc) in gj_of[(h, g)]:
                    sc.wait_ge(amm, cj + 1)
                    if diag:
                        sc.wait_ge(amk, mc + 1)
                    if cj >= 4:
                        sc.wait_ge(apv, cj - 3)
                    sc.activation(pt_sb[:, (cj % 4) * 512:(cj % 4 + 1) * 512],
                                  sp[cj % 3][:, :], AF.Exp,
                                  scale=float(1.0 / np.sqrt(HD))).then_inc(aex, 1)

        @block.sync
        def _(s):
            s.wait_ge(acev, len(groups))
            for h in range(HPC):
                s.dma_start(out=ctx_out[h * 128:(h + 1) * 128, :],
                            in_=ctx_sb[:, h * S:(h + 1) * S]).then_inc(cst, 16)
            s.wait_ge(cst, 32)


_IDENT_CACHE = {}


def nc_ident_ap(nc):
    key = id(nc)
    if key not in _IDENT_CACHE:
        _IDENT_CACHE[key] = nc.inline_tensor(np.eye(128, dtype=BF), name="ident_a")
    return _IDENT_CACHE[key].ap()


def _mask_inline(nc):
    m = np.full((128, 1024), MASKVAL, dtype=np.float32)
    for ki in range(128):
        m[ki, ki + 512:] = 0.0
    return nc.inline_tensor(m, name="mask_master")


def build_attn(debug=False):
    nc = bass.Bass("TRN2", target_bir_lowering=False)
    xn_full = nc.dram_tensor("xn_full", [NCORE * H, TPC], BF16, kind="ExternalInput")
    wqkv = nc.dram_tensor("wqkv", [H, 768], F32, kind="ExternalInput")
    bqkv2 = nc.dram_tensor("bqkv2", [128, 6], F32, kind="ExternalInput")
    ctx_out = nc.dram_tensor("ctx", [HPC * 128, S], BF16, kind="ExternalOutput")
    dbg = None
    if debug:
        dbg = (nc.dram_tensor("dq", [128, HPC * S], BF16, kind="ExternalOutput"),
               nc.dram_tensor("dk", [128, HPC * S], BF16, kind="ExternalOutput"),
               nc.dram_tensor("dv", [128, HPC * S], BF16, kind="ExternalOutput"))
    mask_inl = _mask_inline(nc)
    import contextlib
    with contextlib.ExitStack() as es:
        q_sb = es.enter_context(nc.sbuf_tensor("q_sb", [128, HPC * S], BF16))
        k_sb = es.enter_context(nc.sbuf_tensor("k_sb", [128, HPC * S], BF16))
        v_tm = es.enter_context(nc.sbuf_tensor("v_tm", [128, HPC * S], BF16))
        ctx_sb = es.enter_context(nc.sbuf_tensor("ctx_sb", [128, HPC * S], BF16))
        emit_attention(nc, xn_full, wqkv, bqkv2, ctx_out, mask_inl,
                       q_sb, k_sb, v_tm, ctx_sb, debug_qkv=dbg)
    return nc



def build_l3():
    nc = bass.Bass("TRN2", target_bir_lowering=False)
    ctx_my = nc.dram_tensor("ctx_my", [H, TPC], BF16, kind="ExternalInput")
    x_in = nc.dram_tensor("x", [TPC, H], F32, kind="ExternalInput")
    ffn_in = nc.dram_tensor("ffn", [TPC, H], F32, kind="ExternalInput")
    w_out = nc.dram_tensor("w_out", [H, H], F32, kind="ExternalInput")
    b_out = nc.dram_tensor("b_out", [H], F32, kind="ExternalInput")
    y_out = nc.dram_tensor("y", [TPC, H], F32, kind="ExternalOutput")

    import contextlib
    with contextlib.ExitStack() as es:
        E = es.enter_context
        ctx_sb = E(nc.sbuf_tensor("o_ctx", [128, NFC * TPC], BF16))
        wo_sb = E(nc.sbuf_tensor("o_wo", [128, NFC * H], BF16))
        brow = E(nc.sbuf_tensor("o_brow", [1, H], F32))
        onesk1 = E(nc.sbuf_tensor("o_ones", [1, 128], F32))
        b_bc = E(nc.sbuf_tensor("o_bbc", [128, H], F32))
        xsl = E(nc.sbuf_tensor("o_x", [128, 2 * H], F32))
        fsl = E(nc.sbuf_tensor("o_f", [128, 2 * H], F32))
        osl = E(nc.sbuf_tensor("o_o", [128, 2 * H], F32))
        tsl = E(nc.sbuf_tensor("o_t", [128, 2 * H], F32))
        bp = E(nc.psum_tensor("o_bp", [128, H], F32))
        op0 = E(nc.psum_tensor("o_p0", [128, 512], F32))
        op1 = E(nc.psum_tensor("o_p1", [128, 512], F32))
        ld = E(nc.semaphore("o_ld"))
        wl2 = E(nc.semaphore("o_wl2"))
        xld = [E(nc.semaphore("o_xl0")), E(nc.semaphore("o_xl1"))]
        bmm = E(nc.semaphore("o_bmm"))
        bev = E(nc.semaphore("o_bev"))
        omm = E(nc.semaphore("o_omm"))
        oev = E(nc.semaphore("o_oev"))
        vsy = E(nc.semaphore("o_vsy"))
        ost = [E(nc.semaphore("o_st0")), E(nc.semaphore("o_st1"))]
        op = [op0, op1]

        with nc.Block() as block:
            @block.gpsimd
            def _(g):
                for j in range(4):
                    g.dma_start(
                        out=AP2(wo_sb, j * 4 * H, [[NFC * H, 128], [H, 4], [1, H]]),
                        in_=AP2(w_out, j * 4 * 128 * H, [[H, 128], [128 * H, 4], [1, H]])
                    ).then_inc(wl2, 16)

            @block.sync
            def _(s):
                s.dma_start(out=AP2(ctx_sb, 0, [[NFC * TPC, 128], [TPC, NFC], [1, TPC]]),
                            in_=AP2(ctx_my, 0, [[TPC, 128], [128 * TPC, NFC], [1, TPC]])
                            ).then_inc(ld, 16)
                s.dma_start(out=brow[:, :], in_=b_out[:]).then_inc(ld, 16)
                s.wait_ge(ld, 32)

            @block.vector
            def _(v):
                v.memset(onesk1[:, :], 1.0)

        with nc.Block() as block:
            @block.tensor
            def _(t):
                for j in range(4):
                    t.matmul(bp[:, j * 512:(j + 1) * 512], onesk1[:, :],
                             brow[:, j * 512:(j + 1) * 512],
                             start=True, stop=True).then_inc(bmm, 1)

            @block.vector
            def _(v):
                v.wait_ge(bmm, 4)
                v.tensor_copy(b_bc[:, :], bp[:, :]).then_inc(bev, 1)

        with nc.Block() as block:
            ntc = TPC // 128

            def _store(s, tc):
                sl = tc % 2
                s.dma_start(out=y_out[tc * 128:(tc + 1) * 128, :],
                            in_=osl[:, sl * H:sl * H + H]).then_inc(ost[sl], 16)

            @block.sync
            def _(s):
                for tc in range(ntc):
                    sl = tc % 2
                    if tc >= 2:
                        s.wait_ge(oev, (tc - 1) * 4)
                        _store(s, tc - 2)
                    s.dma_start(out=xsl[:, sl * H:sl * H + H],
                                in_=x_in[tc * 128:(tc + 1) * 128, :]).then_inc(xld[sl], 16)
                    s.dma_start(out=fsl[:, sl * H:sl * H + H],
                                in_=ffn_in[tc * 128:(tc + 1) * 128, :]).then_inc(xld[sl], 16)
                for tc in (ntc - 2, ntc - 1):
                    s.wait_ge(oev, tc * 4 + 4)
                    _store(s, tc)
                s.wait_ge(ost[0], 16 * (ntc // 2))
                s.wait_ge(ost[1], 16 * (ntc // 2))

            @block.tensor
            def _(t):
                t.wait_ge(wl2, 64)
                for tc in range(ntc):
                    for ob in range(4):
                        i = tc * 4 + ob
                        if i >= 2:
                            t.wait_ge(oev, i - 1)
                        for fc in range(NFC):
                            inst = t.matmul(
                                op[i % 2][:, :],
                                ctx_sb[:, fc * TPC + tc * 128: fc * TPC + (tc + 1) * 128],
                                wo_sb[:, fc * H + ob * 512: fc * H + (ob + 1) * 512],
                                start=(fc == 0), stop=(fc == NFC - 1))
                        inst.then_inc(omm, 1)

            @block.vector
            def _(v):
                v.wait_ge(bev, 1)
                vsn = [0]

                def stamp(inst):
                    vsn[0] += 1
                    inst.then_inc(vsy, 1)

                for tc in range(ntc):
                    sl = tc % 2
                    v.wait_ge(xld[sl], 32 * (tc // 2 + 1))
                    for ob in range(4):
                        i = tc * 4 + ob
                        v.wait_ge(omm, i + 1)
                        if tc >= 2:
                            v.wait_ge(ost[sl], 16 * (tc // 2))
                        co = sl * H + ob * 512
                        stamp(v.scalar_tensor_tensor(
                            tsl[:, co:co + 512], op[i % 2][:, :], 1.0,
                            xsl[:, co:co + 512], ALU.mult, ALU.add))
                        v.wait_ge(vsy, vsn[0])
                        stamp(v.scalar_tensor_tensor(
                            osl[:, co:co + 512], tsl[:, co:co + 512], 1.0,
                            fsl[:, co:co + 512], ALU.mult, ALU.add))
                        v.wait_ge(vsy, vsn[0])
                        v.tensor_add(osl[:, co:co + 512], osl[:, co:co + 512],
                                     b_bc[:, ob * 512:(ob + 1) * 512]).then_inc(oev, 1)


    return nc



def kernel(L0_attn_hidden_states, L0_attn_w_qkv, L0_attn_b_qkv, L0_attn_w_out,
           L0_attn_b_out, L0_ffn_SU_ug, L0_ffn_SU_dp, L0_ffn_Wscale_ug,
           L0_ffn_Wscale_dp, L0_ffn_W_up, L0_ffn_W_gate, L0_ffn_W_down):
    x = np.ascontiguousarray(
        np.asarray(L0_attn_hidden_states, dtype=np.float32).reshape(S, H))
    w_qkv = np.asarray(L0_attn_w_qkv, dtype=np.float32)
    b_qkv = np.asarray(L0_attn_b_qkv, dtype=np.float32)
    w_out = np.ascontiguousarray(np.asarray(L0_attn_w_out, dtype=np.float32))
    b_out = np.asarray(L0_attn_b_out, dtype=np.float32)
    su_ug = np.asarray(L0_ffn_SU_ug, dtype=np.float32)
    su_dp = np.asarray(L0_ffn_SU_dp, dtype=np.float32)
    ws_ug = np.asarray(L0_ffn_Wscale_ug, dtype=np.float32)
    ws_dp = np.asarray(L0_ffn_Wscale_dp, dtype=np.float32)
    w_upT = np.ascontiguousarray(np.asarray(L0_ffn_W_up, dtype=np.float32).T)
    w_gateT = np.ascontiguousarray(np.asarray(L0_ffn_W_gate, dtype=np.float32).T)
    w_downT = np.ascontiguousarray(np.asarray(L0_ffn_W_down, dtype=np.float32).T)

    cores = list(range(NCORE))

    nc1 = build_l1()
    res1 = run_bass_kernel_spmd(
        nc1, [{"x": x[c * TPC:(c + 1) * TPC]} for c in cores], core_ids=cores)
    xn_shards = [np.asarray(r["xn_fm"]) for r in res1.results]
    xn_full = np.ascontiguousarray(np.concatenate(xn_shards, axis=0))

    nc2 = build_ffn()
    ffn_maps = []
    for c in cores:
        ffn_maps.append({
            "xn_own": xn_shards[c],
            "su_ug2": np.ascontiguousarray(su_ug.reshape(NFC, 128).T),
            "su_dp2": np.ascontiguousarray(su_dp.reshape(NIC, 128).T),
            "wsu2": np.ascontiguousarray(ws_ug[:INTER].reshape(NIC, 128).T),
            "wsg2": np.ascontiguousarray(ws_ug[INTER:].reshape(NIC, 128).T),
            "ws_dp": ws_dp,
            "w_upT": w_upT, "w_gateT": w_gateT, "w_downT": w_downT,
        })
    res2 = run_bass_kernel_spmd(nc2, ffn_maps, core_ids=cores)
    ffn_shards = [np.asarray(r["ffn_out"]) for r in res2.results]

    nc3 = build_attn()
    attn_maps = []
    for c in cores:
        attn_maps.append({
            "xn_full": xn_full,
            "wqkv": np.ascontiguousarray(w_qkv[:, 768 * c:768 * (c + 1)]),
            "bqkv2": np.ascontiguousarray(
                b_qkv[768 * c:768 * (c + 1)].reshape(6, 128).T),
        })
    res3 = run_bass_kernel_spmd(nc3, attn_maps, core_ids=cores)
    ctx_full = np.concatenate([np.asarray(r["ctx"]) for r in res3.results], axis=0)

    nc4 = build_l3()
    l3_maps = []
    for c in cores:
        l3_maps.append({
            "ctx_my": np.ascontiguousarray(ctx_full[:, c * TPC:(c + 1) * TPC]),
            "x": x[c * TPC:(c + 1) * TPC],
            "ffn": ffn_shards[c],
            "w_out": w_out,
            "b_out": b_out,
        })
    res4 = run_bass_kernel_spmd(nc4, l3_maps, core_ids=cores)
    y = np.concatenate([np.asarray(r["y"]) for r in res4.results], axis=0)
    return y.reshape(1, S, H)
